# version 09
# baseline (speedup 1.0000x reference)
import contextlib
import numpy as np
import concourse.bass as bass
import concourse.mybir as mybir
from concourse.bass_utils import run_bass_kernel_spmd
from concourse.alu_op_type import AluOpType as OP

F32 = mybir.dt.float32
BF16 = mybir.dt.bfloat16
AF = mybir.ActivationFunctionType


class CFG:
    def __init__(self, S=2048, D=2048, HN=16, HF=1408, E=8, B_LOC=2, dump=False,
                 stop="full"):
        self.S, self.D, self.HN, self.HF, self.E, self.B_LOC = S, D, HN, HF, E, B_LOC
        self.stop = stop
        self.DH = 128
        self.RQ, self.RK, self.RV = 6, 2, 2
        self.HS = 2 * HF
        self.dump = dump
        assert S % 256 == 0 and D % 512 == 0 and HF % 128 == 0 and HN * 128 == D
        self.nT = S // 128
        self.nTh = self.nT // 2
        self.Sh = S // 2
        self.nD = D // 128
        self.nDB = D // 512
        self.nHF = HF // 128
        self.nHS = self.HS // 128
        self.WQ = self.RQ * 128
        self.WK = self.RK * 128
        self.WV = self.RV * 128
        self.WA = (self.RQ + self.RK + self.RV) * HN
        self.WCAT = self.WQ + self.WK + self.WV + self.WA
        self.SBLK = min(512, S)
        self.nSB = S // self.SBLK


ENGS = ("tensor", "vector", "scalar", "sync", "gpsimd")
DMA_ENGS = ("sync", "gpsimd")


class Em:

    def __init__(self):
        self.steps = []
        self.writers = {}
        self.readers = {}

    def op(self, eng, emit, r=(), w=()):
        deps = {}
        for k in r:
            for d in self.writers.get(k, ()):
                deps[d] = "RAW"
        for k in w:
            for d in self.writers.get(k, ()):
                if deps.get(d) != "RAW":
                    deps[d] = "WAW"
            for rd in self.readers.get(k, ()):
                if rd not in deps:
                    deps[rd] = "WAR"
        keep = []
        for d, ty in deps.items():
            deng = self.steps[d][0]
            if deng == eng:
                if eng == "tensor" or eng in DMA_ENGS:
                    continue
            keep.append(d)
        i = len(self.steps)
        self.steps.append([eng, emit, keep, False])
        for k in w:
            wl = self.writers.setdefault(k, [])
            if eng not in DMA_ENGS:
                wl[:] = [d for d in wl if self.steps[d][0] != eng]
            wl.append(i)
            self.readers[k] = []
        for k in r:
            rl = self.readers.setdefault(k, [])
            if eng not in DMA_ENGS:
                rl[:] = [d for d in rl if self.steps[d][0] != eng]
            rl.append(i)
        return i

    def t(self, emit, r=(), w=()):
        return self.op("tensor", emit, r, w)

    def v(self, emit, r=(), w=()):
        return self.op("vector", emit, r, w)

    def s(self, emit, r=(), w=()):
        return self.op("scalar", emit, r, w)

    def q1(self, emit, r=(), w=()):
        return self.op("sync", emit, r, w)

    def q2(self, emit, r=(), w=()):
        return self.op("gpsimd", emit, r, w)

    NLANES = 8

    def finalize(self, blk, sems):
        for st in self.steps:
            if st[0] in DMA_ENGS:
                st[3] = True
            for d in st[2]:
                self.steps[d][3] = True
        lane_of = {}
        qcount = {q: 0 for q in DMA_ENGS}
        evno = {}
        counts = {}
        for i, st in enumerate(self.steps):
            if st[0] in DMA_ENGS:
                lane = qcount[st[0]] % self.NLANES
                qcount[st[0]] += 1
                key = (st[0], lane)
                lane_of[i] = key
            else:
                if not st[3]:
                    continue
                key = st[0]
            counts[key] = counts.get(key, 0) + 1
            evno[i] = counts[key]
        run_lane = {}
        prefix_at = {}
        for i, st in enumerate(self.steps):
            if st[0] in DMA_ENGS:
                k = lane_of[i]
                run_lane[k] = run_lane.get(k, 0) + 1
                prefix_at[i] = run_lane[k]
        waited = {}
        waits = [[] for _ in self.steps]
        lane_seen = {}
        cur_lane = {}
        for i, st in enumerate(self.steps):
            if st[0] in DMA_ENGS:
                cur_lane[lane_of[i]] = prefix_at[i]
            eng = st[0]
            for d in sorted(st[2]):
                dst = self.steps[d]
                if dst[0] in DMA_ENGS:
                    key = lane_of[d]
                    val = 16 * cur_lane[key]
                else:
                    key = dst[0]
                    val = evno[d]
                kk = (eng, i if False else key)
                kk = (eng, key)
                if waited.get(kk, 0) >= val:
                    continue
                waited[kk] = val
                waits[i].append((key, val))
        per_eng = {e: [] for e in ENGS}
        for i, st in enumerate(self.steps):
            per_eng[st[0]].append(i)

        def runner(ename):
            def _(engine):
                for i in per_eng[ename]:
                    st = self.steps[i]
                    for (key, val) in waits[i]:
                        engine.wait_ge(sems[key], val)
                    inst = st[1](engine)
                    if st[3]:
                        if st[0] in DMA_ENGS:
                            inst.then_inc(sems[lane_of[i]], 16)
                        else:
                            inst.then_inc(sems[st[0]], 1)
            return _

        for ename in ENGS:
            getattr(blk, ename)(runner(ename))
        return counts


def kspan(name, a, b):
    return [(name, j) for j in range(a // 512, (b - 1) // 512 + 1)]


def nsplit(a, b, step=512):
    out = []
    while a < b:
        out.append((a, min(a + step, b)))
        a = min(a + step, b)
    return out


def build_program(cfg: CFG):
    c = cfg
    nc = bass.Bass("TRN2", target_bir_lowering=False)

    x_d = nc.dram_tensor("x", [c.B_LOC, c.S, c.D], F32, kind="ExternalInput").ap()
    cf_d = nc.dram_tensor("cfreq", [1, c.S + 1, 1, 64, 2], F32, kind="ExternalInput").ap()
    wAq_d = nc.dram_tensor("wAq", [c.D, c.RQ * c.HN], F32, kind="ExternalInput").ap()
    wAk_d = nc.dram_tensor("wAk", [c.D, c.RK * c.HN], F32, kind="ExternalInput").ap()
    wAv_d = nc.dram_tensor("wAv", [c.D, c.RV * c.HN], F32, kind="ExternalInput").ap()
    wBq_d = nc.dram_tensor("wBq", [c.D, c.WQ], F32, kind="ExternalInput").ap()
    wBk_d = nc.dram_tensor("wBk", [c.D, c.WK], F32, kind="ExternalInput").ap()
    wBv_d = nc.dram_tensor("wBv", [c.D, c.WV], F32, kind="ExternalInput").ap()
    wo_d = nc.dram_tensor("wo", [c.D, c.D], F32, kind="ExternalInput").ap()
    rw_d = nc.dram_tensor("rw", [c.E, c.D], F32, kind="ExternalInput").ap()
    wg_d = nc.dram_tensor("wg", [c.E, c.D, c.HF], F32, kind="ExternalInput").ap()
    wu_d = nc.dram_tensor("wu", [c.E, c.D, c.HF], F32, kind="ExternalInput").ap()
    wd_d = nc.dram_tensor("wd", [c.E, c.HF, c.D], F32, kind="ExternalInput").ap()
    sg_d = nc.dram_tensor("sg", [c.D, c.HS], F32, kind="ExternalInput").ap()
    su_d = nc.dram_tensor("su", [c.D, c.HS], F32, kind="ExternalInput").ap()
    sd_d = nc.dram_tensor("sd", [c.HS, c.D], F32, kind="ExternalInput").ap()
    identb_d = nc.dram_tensor("identb", [128, 128], BF16, kind="ExternalInput").ap()
    identf_d = nc.dram_tensor("identf", [128, 128], F32, kind="ExternalInput").ap()
    wg_s = nc.dram_tensor("wg_s", [c.E + 2, c.HF // 128, 128, c.D], BF16, kind="Internal").ap()
    wu_s = nc.dram_tensor("wu_s", [c.E + 2, c.HF // 128, 128, c.D], BF16, kind="Internal").ap()
    wd_s = nc.dram_tensor("wd_s", [c.E + 2, c.D // 512, c.HF // 128, 128, 512], BF16,
                          kind="Internal").ap()
    wcat_s = nc.dram_tensor("wcat_s", [c.D // 128, 128, c.WCAT], BF16, kind="Internal").ap()
    wo_s = nc.dram_tensor("wo_s", [c.D // 128, 128, c.D], BF16, kind="Internal").ap()
    out_d = nc.dram_tensor("out", [c.B_LOC, c.S, c.D], F32, kind="ExternalOutput").ap()

    dumps = {}
    if c.dump:
        def dmp(name, shape, dtype=F32):
            dumps[name] = nc.dram_tensor(name, shape, dtype, kind="ExternalOutput").ap()
        dmp("d_bq", [128, c.nT * c.WQ], BF16)
        dmp("d_bv", [128, c.nT * c.WV], BF16)
        dmp("d_a32", [128, c.nT * c.WA])
        dmp("d_qt", [128, c.S], BF16)
        dmp("d_kt", [128, c.S], BF16)
        dmp("d_vx", [128, (c.S // 128) * 132], BF16)
        dmp("d_pt", [128, (c.S // 128) * c.SBLK], BF16)
        dmp("d_o", [128, c.nT * c.D], BF16)
        dmp("d_x1", [128, c.nT * c.D], BF16)
        dmp("d_gates", [128, c.nT * 8])
        dmp("d_h0", [128, c.nHF * c.Sh], BF16)

    VW = (c.S // 128) * 132
    WPASS = 384
    A512 = lambda n: (n + 511) // 512 * 512
    CONST_N = 2560
    A1_N = A512(max(c.nD * c.Sh, 2 * (c.S // 128) * c.SBLK, 2 * c.nT * 128))
    WSEG_N = A512(max(2 * c.D + c.nD * WPASS + 1024,
                      2 * c.S + 2 * c.S + VW,
                      4 * (c.nD * 128) + 2 * 2048))
    BQ_N = A512(max(c.nT * (c.WQ + c.WK + c.WV) + A512(2 * c.nT * c.WA),
                    c.nHF * c.Sh) + 2 * 512)
    A2_N = c.nT * c.D
    assert WSEG_N + BQ_N >= c.nD * c.D, "Wo must fit in WSEG+BQ"
    seg = {}
    off = 0
    for name, n in (("CONST", CONST_N), ("A1", A1_N), ("WSEG", WSEG_N),
                    ("BQ", BQ_N), ("A2", A2_N)):
        assert off % 512 == 0 and n % 512 == 0, (name, off, n)
        seg[name] = (off, n)
        off += n
    NBF = off
    XL_N = 2 * c.D
    WL_N = c.D + 1024
    MISC_N = 512
    NF32 = XL_N + WL_N + MISC_N
    XL0, WL0, MI0 = 0, XL_N, XL_N + WL_N

    es = contextlib.ExitStack()
    with es:
        AR = es.enter_context(nc.sbuf_tensor("AR", [128, NBF], BF16))
        AFr = es.enter_context(nc.sbuf_tensor("AF32", [128, NF32], F32))
        ps = [es.enter_context(nc.psum_tensor(f"ps{i}", [128, 512], F32)) for i in range(8)]
        sems = {e: es.enter_context(nc.semaphore(f"sem_{e}")) for e in
                ("tensor", "vector", "scalar")}
        for q in DMA_ENGS:
            for l in range(Em.NLANES):
                sems[(q, l)] = es.enter_context(nc.semaphore(f"sem_{q}{l}"))
        blk = es.enter_context(nc.Block())
        em = Em()

        def bv(sn, a, b):
            o, n = seg[sn]
            assert 0 <= a <= b <= n, (sn, a, b, n)
            return AR[:, o + a:o + b]

        def bk(sn, a, b):
            o, _ = seg[sn]
            return kspan("AR", o + a, o + b)

        def fv(a, b):
            return AFr[:, a:b]

        def fk(a, b):
            return kspan("AF", 2 * a, 2 * b)

        PS = lambda i: ps[i]
        PSK = lambda i: [("PS", i)]

        mi_bn = MI0
        mi_mv = MI0 + 48
        mi_rstd = MI0 + 56
        mi_eps = MI0 + 60
        mi_recip = MI0 + 64
        mi_row = MI0 + 72
        mi_eg = MI0 + 128
        IDB = bv("CONST", 0, 128); k_idb = bk("CONST", 0, 128)
        IDF = bv("CONST", 128, 384).bitcast(F32); k_idf = bk("CONST", 128, 384)
        FRB = bv("CONST", 512, 896); k_frb = bk("CONST", 512, 896)
        FIB = bv("CONST", 1024, 1408); k_fib = bk("CONST", 1024, 1408)
        assert c.nT * 16 <= 512
        g0 = 1536
        k_gates_all = bk("CONST", g0, g0 + c.nT * 16)
        qs0 = 2048
        QKST = bv("CONST", qs0, qs0 + 512)

        def gates_v(i):
            return bv("CONST", g0 + i * 16, g0 + (i + 1) * 16).bitcast(F32)

        def gates_k(i):
            return bk("CONST", g0 + i * 16, g0 + (i + 1) * 16)

        em.q2(lambda g: g.dma_start(IDB, identb_d[:, :]), w=k_idb)
        em.q2(lambda g: g.dma_start(IDF, identf_d[:, :]), w=k_idf)
        em.v(lambda v: v.memset(fv(mi_eps, mi_eps + 1), 1e-5), w=fk(mi_eps, mi_eps + 1))
        frow = fv(MI0 + 256, MI0 + 384)[0:1, :]
        k_frow = fk(MI0 + 256, MI0 + 384)
        em.q2(lambda g: g.dma_start(frow, cf_d[0:1, c.S, 0, :, :].rearrange("o a b -> o (a b)")),
              w=k_frow)
        frep = fv(WL0, WL0 + 384)[0:1, :]
        firep = fv(WL0 + 384, WL0 + 768)[0:1, :]
        k_frep = fk(WL0, WL0 + 384)
        k_firep = fk(WL0 + 384, WL0 + 768)
        for r in range(c.RQ):
            em.v(lambda v, r=r: v.tensor_scalar(frep[:, r * 64:(r + 1) * 64],
                                                frow[:, 0:128:2], 1.0, None, OP.mult),
                 r=k_frow, w=k_frep)
            em.v(lambda v, r=r: v.tensor_scalar(firep[:, r * 64:(r + 1) * 64],
                                                frow[:, 1:128:2], 1.0, None, OP.mult),
                 r=k_frow, w=k_firep)
        ones_row = fv(WL0 + 768, WL0 + 896)[0:1, :]
        k_ones = fk(WL0 + 768, WL0 + 896)
        em.v(lambda v: v.memset(ones_row, 1.0), w=k_ones)
        em.t(lambda t: t.matmul(PS(0)[:, 0:384], ones_row, frep, start=True, stop=True),
             r=k_ones + k_frep, w=PSK(0))
        em.v(lambda v: v.tensor_copy(FRB, PS(0)[:, 0:384]), r=PSK(0), w=k_frb)
        em.t(lambda t: t.matmul(PS(1)[:, 0:384], ones_row, firep, start=True, stop=True),
             r=k_ones + k_firep, w=PSK(1))
        em.v(lambda v: v.tensor_copy(FIB, PS(1)[:, 0:384]), r=PSK(1), w=k_fib)
        rwt0 = MI0 + 384
        assert 384 + c.nD * 8 <= MISC_N
        RWT = lambda ch: fv(rwt0 + ch * 8, rwt0 + (ch + 1) * 8)
        k_rwt = fk(rwt0, rwt0 + c.nD * 8)
        rwr0 = XL0
        rwrow = fv(rwr0, rwr0 + c.D)[0:c.E, :]
        k_rwrow = fk(rwr0, rwr0 + c.D)
        em.q2(lambda g: g.dma_start(rwrow, rw_d[:, :]), w=k_rwrow)
        for ch in range(c.nD):
            em.t(lambda t, ch=ch: t.matmul(PS(2)[:, 0:8], rwrow[:, ch * 128:(ch + 1) * 128],
                                           IDF[0:c.E, 0:8], start=True, stop=True),
                 r=k_rwrow + k_idf, w=PSK(2))
            em.v(lambda v, ch=ch: v.tensor_copy(RWT(ch), PS(2)[:, 0:8]), r=PSK(2), w=k_rwt)
        vx0 = 4 * c.S

        bq0 = 0
        bk0 = c.nT * c.WQ
        bv0_ = bk0 + c.nT * c.WK
        a320 = bv0_ + c.nT * c.WV
        sil0 = (a320 + 2 * c.nT * c.WA + 511) // 512 * 512

        def A32v(i):
            return bv("BQ", a320 + 2 * i * c.WA, a320 + 2 * (i + 1) * c.WA).bitcast(F32)

        def A32k(i):
            return bk("BQ", a320 + 2 * i * c.WA, a320 + 2 * (i + 1) * c.WA)

        XLv = lambda p: fv(XL0 + p * c.D, XL0 + (p + 1) * c.D)
        XLk = lambda p: fk(XL0 + p * c.D, XL0 + (p + 1) * c.D)

        def ln_stats(xt, kx, p):
            bn = fv(mi_bn + 24 * p, mi_bn + 24 * (p + 1))
            kbn = fk(mi_bn + 24 * p, mi_bn + 24 * (p + 1))
            mv = fv(mi_mv + 2 * p, mi_mv + 2 * (p + 1))
            kmv = fk(mi_mv + 2 * p, mi_mv + 2 * (p + 1))
            rstd = fv(mi_rstd + p, mi_rstd + p + 1)
            krstd = fk(mi_rstd + p, mi_rstd + p + 1)
            nch = (c.D + 511) // 512
            for ci in range(nch):
                em.v(lambda v, ci=ci, xt=xt, bn=bn: v.bn_stats(
                    bn[:, 6 * ci:6 * ci + 6], xt[:, 512 * ci:min(512 * (ci + 1), c.D)]),
                    r=kx, w=kbn)
            em.v(lambda v, bn=bn, mv=mv, nch=nch: v.bn_aggr(mv, bn[:, 0:6 * nch]),
                 r=kbn, w=kmv)
            em.s(lambda s, mv=mv, rstd=rstd: s.activation(
                rstd, mv[:, 1:2], AF.Ln, bias=fv(mi_eps, mi_eps + 1)),
                r=kmv + fk(mi_eps, mi_eps + 1), w=krstd)
            em.s(lambda s, rstd=rstd: s.activation(rstd, rstd, AF.Exp, scale=-0.5),
                 r=krstd, w=krstd)
            return mv, kmv, rstd, krstd

        n_out_dma = [0]

        def evac_proj(bank, na, nb_, i):
            kps = PSK(bank)
            for (lo, hi, dst0, w_per_tile, roped, rbase) in (
                    (0, c.WQ, bq0, c.WQ, True, 0),
                    (c.WQ, c.WQ + c.WK, bk0, c.WK, True, 0),
                    (c.WQ + c.WK, c.WQ + c.WK + c.WV, bv0_, c.WV, False, 0)):
                a = max(na, lo)
                b2 = min(nb_, hi)
                if a >= b2:
                    continue
                pa = a - na
                n = b2 - a
                da = dst0 + i * w_per_tile + (a - lo)
                if not roped:
                    em.v(lambda v, bank=bank, pa=pa, n=n, da=da: v.tensor_copy(
                        bv("BQ", da, da + n), PS(bank)[:, pa:pa + n]),
                        r=kps, w=bk("BQ", da, da + n))
                    continue
                assert (a - lo) % 128 == 0 and n % 128 == 0
                r0 = (a - lo) // 128
                nr = n // 128
                f0 = r0 * 64 if lo == 0 else r0 * 64
                sc0 = WSEG_N - 1024
                sA = bv("WSEG", sc0, sc0 + nr * 64)
                sB = bv("WSEG", sc0 + 512, sc0 + 512 + nr * 64)
                ksc = bk("WSEG", sc0, sc0 + 1024)
                dst3 = bv("BQ", da, da + n).rearrange("p (r j) -> p r j", j=128)
                kd = bk("BQ", da, da + n)
                em.v(lambda v, bank=bank, pa=pa, n=n, sA=sA, f0=f0, nr=nr: v.tensor_tensor(
                    sA, PS(bank)[:, pa:pa + n:2], FRB[:, f0:f0 + nr * 64], OP.mult),
                    r=kps + k_frb, w=ksc)
                em.v(lambda v, bank=bank, pa=pa, n=n, sB=sB, f0=f0, nr=nr: v.tensor_tensor(
                    sB, PS(bank)[:, pa + 1:pa + n:2], FIB[:, f0:f0 + nr * 64], OP.mult),
                    r=kps + k_fib, w=ksc)
                em.v(lambda v, sA=sA, sB=sB, dst3=dst3: v.tensor_tensor(
                    dst3[:, :, 0:64], sA, sB, OP.subtract), r=ksc, w=kd)
                em.v(lambda v, bank=bank, pa=pa, n=n, sA=sA, f0=f0, nr=nr: v.tensor_tensor(
                    sA, PS(bank)[:, pa:pa + n:2], FIB[:, f0:f0 + nr * 64], OP.mult),
                    r=kps + k_fib, w=ksc)
                em.v(lambda v, bank=bank, pa=pa, n=n, sB=sB, f0=f0, nr=nr: v.tensor_tensor(
                    sB, PS(bank)[:, pa + 1:pa + n:2], FRB[:, f0:f0 + nr * 64], OP.mult),
                    r=kps + k_frb, w=ksc)
                em.v(lambda v, sA=sA, sB=sB, dst3=dst3: v.tensor_tensor(
                    dst3[:, :, 64:128], sA, sB, OP.add), r=ksc, w=kd)
            aa = max(na, c.WQ + c.WK + c.WV)
            if nb_ > aa:
                pa = aa - na
                n = nb_ - aa
                sa = aa - (c.WQ + c.WK + c.WV)
                em.v(lambda v, bank=bank, pa=pa, n=n, i=i, sa=sa: v.tensor_copy(
                    A32v(i)[:, sa:sa + n], PS(bank)[:, pa:pa + n]),
                    r=kps, w=A32k(i))

        for b in range(c.B_LOC):
            for th in range(2):
                for ii in range(c.nTh):
                    i = th * c.nTh + ii
                    p = ii % 2
                    xt = XLv(p)
                    em.q2(lambda g, i=i, xt=xt, b=b: g.dma_start(
                        xt, x_d[b, i * 128:(i + 1) * 128, :]), w=XLk(p))
                    mv, kmv, rstd, krstd = ln_stats(xt, XLk(p), p)
                    xnst = bv("WSEG", p * c.D, (p + 1) * c.D)
                    kxnst = bk("WSEG", p * c.D, (p + 1) * c.D)
                    em.v(lambda v, xt=xt, xnst=xnst, mv=mv, rstd=rstd: v.tensor_scalar(
                        xnst, xt, mv[:, 0:1], rstd, OP.subtract, OP.mult),
                        r=XLk(p) + kmv + krstd, w=kxnst)
                    for ch in range(c.nD):
                        bank = 6 + (ch % 2)
                        em.t(lambda t, xnst=xnst, ch=ch, bank=bank: t.matmul(
                            PS(bank)[:, 0:128], xnst[:, ch * 128:(ch + 1) * 128], IDB,
                            start=True, stop=True), r=kxnst + k_idb, w=PSK(bank))
                        da = ch * c.Sh + ii * 128
                        em.s(lambda s, da=da, bank=bank: s.copy(
                            bv("A1", da, da + 128), PS(bank)[:, 0:128]),
                            r=PSK(bank), w=bk("A1", da, da + 128))
                wsc_q = 1.0 / (c.RQ * np.sqrt(128.0))
                wsrcs = ((wBq_d, 0, c.WQ, 1.0), (wBk_d, c.WQ, c.WK, 1.0),
                         (wBv_d, c.WQ + c.WK, c.WV, 1.0),
                         (wAq_d, c.WQ + c.WK + c.WV, c.RQ * c.HN, wsc_q),
                         (wAk_d, c.WQ + c.WK + c.WV + c.RQ * c.HN, c.RK * c.HN, 1.0 / c.RK),
                         (wAv_d, c.WQ + c.WK + c.WV + (c.RQ + c.RK) * c.HN, c.RV * c.HN,
                          1.0 / c.RV))
                for (wlo, whi) in nsplit(0, c.WCAT, WPASS):
                    ww = whi - wlo
                    wc0 = 2 * c.D
                    first_wcat = (b == 0 and th == 0)
                    for ch in range(c.nD):
                        wcb = bv("WSEG", wc0 + ch * ww, wc0 + (ch + 1) * ww)
                        kwcb = bk("WSEG", wc0 + ch * ww, wc0 + (ch + 1) * ww)
                        kscr = [("WCS", ch, wlo)]
                        if not first_wcat:
                            em.q1(lambda g, ch=ch, wlo=wlo, whi=whi, wcb=wcb: g.dma_start(
                                wcb, wcat_s[ch, :, wlo:whi]), r=kscr, w=kwcb)
                            continue
                        wl = fv(WL0 + c.D, WL0 + c.D + ww)
                        kwl = fk(WL0 + c.D, WL0 + c.D + ww)
                        for (srcd, w0, wn, scv) in wsrcs:
                            a = max(wlo, w0)
                            b2 = min(whi, w0 + wn)
                            if a >= b2:
                                continue
                            s0, n, d0 = a - w0, b2 - a, a - wlo
                            em.q1(lambda g, srcd=srcd, s0=s0, n=n, d0=d0, ch=ch, wl=wl:
                                  g.dma_start(wl[:, d0:d0 + n],
                                              srcd[ch * 128:(ch + 1) * 128, s0:s0 + n]),
                                  w=fk(WL0 + c.D + d0, WL0 + c.D + d0 + n))
                        for (srcd, w0, wn, scv) in wsrcs:
                            a = max(wlo, w0)
                            b2 = min(whi, w0 + wn)
                            if a >= b2:
                                continue
                            d0, n = a - wlo, b2 - a
                            if scv == 1.0:
                                em.v(lambda v, wl=wl, wcb=wcb, d0=d0, n=n: v.tensor_copy(
                                    wcb[:, d0:d0 + n], wl[:, d0:d0 + n]), r=kwl, w=kwcb)
                            else:
                                em.v(lambda v, wl=wl, wcb=wcb, d0=d0, n=n, scv=scv:
                                     v.tensor_scalar(wcb[:, d0:d0 + n], wl[:, d0:d0 + n],
                                                     scv, None, OP.mult), r=kwl, w=kwcb)
                        em.q2(lambda g, ch=ch, wlo=wlo, whi=whi, wcb=wcb: g.dma_start(
                            wcat_s[ch, :, wlo:whi], wcb), r=kwcb, w=kscr)
                    for ii in range(c.nTh):
                        i = th * c.nTh + ii
                        bank = 2 + (ii % 3)
                        for ch in range(c.nD):
                            em.t(lambda t, ch=ch, ww=ww, ii=ii, bank=bank, wc0=wc0: t.matmul(
                                PS(bank)[:, 0:ww],
                                bv("A1", ch * c.Sh + ii * 128, ch * c.Sh + (ii + 1) * 128),
                                bv("WSEG", wc0 + ch * ww, wc0 + (ch + 1) * ww),
                                start=(ch == 0), stop=(ch == c.nD - 1)),
                                r=bk("A1", ch * c.Sh + ii * 128, ch * c.Sh + (ii + 1) * 128)
                                  + bk("WSEG", wc0 + ch * ww, wc0 + (ch + 1) * ww),
                                w=PSK(bank))
                        evac_proj(bank, wlo, whi, i)
            if c.dump and b == 0:
                em.q2(lambda g: g.dma_start(dumps["d_bq"][:, :], bv("BQ", bq0, bq0 + c.nT * c.WQ)),
                      r=bk("BQ", bq0, bq0 + c.nT * c.WQ))
                em.q2(lambda g: g.dma_start(dumps["d_bv"][:, :],
                                            bv("BQ", bv0_, bv0_ + c.nT * c.WV)),
                      r=bk("BQ", bv0_, bv0_ + c.nT * c.WV))
                em.q2(lambda g: g.dma_start(
                    dumps["d_a32"][:, :],
                    bv("BQ", a320, a320 + 2 * c.nT * c.WA).bitcast(F32)),
                    r=bk("BQ", a320, a320 + 2 * c.nT * c.WA))

            if c.stop == "A":
                break
            ap_ones = bv("WSEG", vx0, vx0 + VW).rearrange(
                "p (t w) -> p t w", w=132)[:, :, 128:129]
            em.v(lambda v, ap=ap_ones: v.memset(ap, 1.0), w=bk("WSEG", vx0, vx0 + VW))
            qt0, kt0 = 0, 2 * c.S
            nTc = c.S // 128
            qk4_0 = (4 * c.S + VW + 511) // 512 * 512
            assert qk4_0 + 3 * 512 <= WSEG_N, (qk4_0, WSEG_N)

            def qk_form_ops(h):
                par = h % 2
                ops = []
                for i in range(c.nT):
                    sl = qk4_0 + (i % 3) * 512
                    qst = bv("WSEG", sl, sl + 128)
                    kst = bv("WSEG", sl + 128, sl + 256)
                    kqst = bk("WSEG", sl, sl + 512)
                    kBq = bk("BQ", bq0 + i * c.WQ, bq0 + (i + 1) * c.WQ)
                    kBk = bk("BQ", bk0 + i * c.WK, bk0 + (i + 1) * c.WK)
                    kA = A32k(i)
                    aqc = lambda r, i=i, h=h: A32v(i)[:, r * c.HN + h:r * c.HN + h + 1]
                    akc = lambda r, i=i, h=h: A32v(i)[:, c.RQ * c.HN + r * c.HN + h:
                                                      c.RQ * c.HN + r * c.HN + h + 1]
                    Bqt = lambda r, i=i: bv("BQ", bq0 + i * c.WQ + r * 128,
                                            bq0 + i * c.WQ + (r + 1) * 128)
                    Bkt = lambda r, i=i: bv("BQ", bk0 + i * c.WK + r * 128,
                                            bk0 + i * c.WK + (r + 1) * 128)
                    tile = []
                    tile.append(("vector", (lambda v, qst=qst, Bqt=Bqt, aqc=aqc:
                                            v.tensor_scalar(qst, Bqt(0), aqc(0), None,
                                                            OP.mult)), kBq + kA, kqst))
                    tile.append(("vector", (lambda v, kst=kst, Bkt=Bkt, akc=akc:
                                            v.tensor_scalar(kst, Bkt(0), akc(0), None,
                                                            OP.mult)), kBk + kA, kqst))
                    for r in range(1, max(c.RQ, c.RK)):
                        if r < c.RQ:
                            tile.append(("vector", (lambda v, qst=qst, Bqt=Bqt, aqc=aqc, r=r:
                                                    v.scalar_tensor_tensor(
                                                        qst, Bqt(r), aqc(r), qst, OP.mult,
                                                        OP.add)), kBq + kA, kqst))
                        if r < c.RK:
                            tile.append(("vector", (lambda v, kst=kst, Bkt=Bkt, akc=akc, r=r:
                                                    v.scalar_tensor_tensor(
                                                        kst, Bkt(r), akc(r), kst, OP.mult,
                                                        OP.add)), kBk + kA, kqst))
                    ops.append(tile)
                return ops

            def qk_transpose_ops(h):
                par = h % 2
                QT = bv("WSEG", qt0 + par * c.S, qt0 + (par + 1) * c.S)
                KT = bv("WSEG", kt0 + par * c.S, kt0 + (par + 1) * c.S)
                kQT = bk("WSEG", qt0 + par * c.S, qt0 + (par + 1) * c.S)
                kKT = bk("WSEG", kt0 + par * c.S, kt0 + (par + 1) * c.S)
                ops = []
                for i in range(c.nT):
                    sl = qk4_0 + (i % 3) * 512
                    qst = bv("WSEG", sl, sl + 128)
                    kst = bv("WSEG", sl + 128, sl + 256)
                    kqst = bk("WSEG", sl, sl + 512)
                    bank = 6 + (i % 2)
                    tile = [
                        ("tensor", (lambda t, qst=qst, bank=bank: t.matmul(
                            PS(bank)[:, 0:128], qst, IDB, start=True, stop=True)),
                         kqst + k_idb, PSK(bank)),
                        ("scalar", (lambda s, QT=QT, i=i, bank=bank: s.copy(
                            QT[:, i * 128:(i + 1) * 128], PS(bank)[:, 0:128])),
                         PSK(bank), kQT),
                        ("tensor", (lambda t, kst=kst, bank=bank: t.matmul(
                            PS(bank)[:, 128:256], kst, IDB, start=True, stop=True)),
                         kqst + k_idb, PSK(bank)),
                        ("scalar", (lambda s, KT=KT, i=i, bank=bank: s.copy(
                            KT[:, i * 128:(i + 1) * 128], PS(bank)[:, 128:256])),
                         PSK(bank), kKT),
                    ]
                    ops.append(tile)
                return ops

            def v_form_ops(h):
                ops = []
                for i in range(c.nT):
                    kBv = bk("BQ", bv0_ + i * c.WV, bv0_ + (i + 1) * c.WV)
                    kA = A32k(i)
                    avc = lambda r, i=i, h=h: A32v(i)[:, (c.RQ + c.RK) * c.HN + r * c.HN + h:
                                                      (c.RQ + c.RK) * c.HN + r * c.HN + h + 1]
                    Bvt = lambda r, i=i: bv("BQ", bv0_ + i * c.WV + r * 128,
                                            bv0_ + i * c.WV + (r + 1) * 128)
                    vxt = bv("WSEG", vx0 + i * 132, vx0 + i * 132 + 128)
                    kvxt = bk("WSEG", vx0 + i * 132, vx0 + i * 132 + 132)
                    tile = [("vector", (lambda v, vxt=vxt, Bvt=Bvt, avc=avc:
                                        v.tensor_scalar(vxt, Bvt(0), avc(0), None, OP.mult)),
                             kBv + kA, kvxt)]
                    for r in range(1, c.RV):
                        tile.append(("vector", (lambda v, vxt=vxt, Bvt=Bvt, avc=avc, r=r:
                                                v.scalar_tensor_tensor(
                                                    vxt, Bvt(r), avc(r), vxt, OP.mult,
                                                    OP.add)), kBv + kA, kvxt))
                    ops.append(tile)
                return ops

            def emit_tiles(tiles, width=2):
                for base in range(0, len(tiles), width):
                    grp = tiles[base:base + width]
                    mx = max(len(t) for t in grp)
                    for k in range(mx):
                        for t in grp:
                            if k < len(t):
                                eng, emit, r, w = t[k]
                                em.op(eng, emit, r, w)

            def attn_core(h, inject=None):
                par = h % 2
                QT = bv("WSEG", qt0 + par * c.S, qt0 + (par + 1) * c.S)
                KT = bv("WSEG", kt0 + par * c.S, kt0 + (par + 1) * c.S)
                kQT = bk("WSEG", qt0 + par * c.S, qt0 + (par + 1) * c.S)
                kKT = bk("WSEG", kt0 + par * c.S, kt0 + (par + 1) * c.S)
                for j in range(c.nSB):
                    jpar = j % 2
                    PT0 = jpar * nTc * c.SBLK
                    for ci in range(nTc):
                        bank = ci % 4
                        em.t(lambda t, KT=KT, QT=QT, ci=ci, j=j, bank=bank: t.matmul(
                            PS(bank)[:, 0:c.SBLK], KT[:, ci * 128:(ci + 1) * 128],
                            QT[:, j * c.SBLK:(j + 1) * c.SBLK], start=True, stop=True),
                            r=kQT + kKT, w=PSK(bank))
                        dst = bv("A1", PT0 + ci * c.SBLK, PT0 + (ci + 1) * c.SBLK)
                        em.s(lambda s, dst=dst, bank=bank: s.activation(
                            dst, PS(bank)[:, 0:c.SBLK], AF.Exp),
                            r=PSK(bank), w=bk("A1", PT0 + ci * c.SBLK, PT0 + (ci + 1) * c.SBLK))
                    if c.dump and b == 0 and h == 0 and j == 0:
                        nP = (c.S // 128) * c.SBLK
                        em.q2(lambda g, PT0=PT0, nP=nP: g.dma_start(
                            dumps["d_pt"][:, :], bv("A1", PT0, PT0 + nP)),
                            r=bk("A1", PT0, PT0 + nP))
                    for k in range(c.SBLK // 128):
                        obank = 4 + (k % 2)
                        for ci in range(nTc):
                            em.t(lambda t, ci=ci, k=k, PT0=PT0, obank=obank: t.matmul(
                                PS(obank)[:, 0:129],
                                bv("A1", PT0 + ci * c.SBLK + k * 128,
                                   PT0 + ci * c.SBLK + (k + 1) * 128),
                                bv("WSEG", vx0 + ci * 132, vx0 + ci * 132 + 129),
                                start=(ci == 0), stop=(ci == nTc - 1)),
                                r=bk("A1", PT0 + ci * c.SBLK + k * 128,
                                     PT0 + ci * c.SBLK + (k + 1) * 128)
                                  + bk("WSEG", vx0 + ci * 132, vx0 + ci * 132 + 132),
                                w=PSK(obank))
                        i_s = j * (c.SBLK // 128) + k
                        rp = fv(mi_recip + k % 2, mi_recip + k % 2 + 1)
                        krp = fk(mi_recip + k % 2, mi_recip + k % 2 + 1)
                        em.v(lambda v, rp=rp, obank=obank: v.reciprocal(
                            rp, PS(obank)[:, 128:129]), r=PSK(obank), w=krp)
                        da = i_s * c.D + h * 128
                        em.v(lambda v, da=da, rp=rp, obank=obank: v.tensor_scalar(
                            bv("A2", da, da + 128), PS(obank)[:, 0:128], rp, None, OP.mult),
                            r=PSK(obank) + krp, w=bk("A2", da, da + 128))
                    if j == 0 and inject is not None:
                        inject()

            def qk_all_ops(h):
                return [f + t for f, t in zip(qk_form_ops(h), qk_transpose_ops(h))]

            emit_tiles(qk_all_ops(0), width=3)
            emit_tiles(v_form_ops(0))
            for h in range(c.HN):
                if c.dump and b == 0 and h == 0:
                    par = 0
                    em.q2(lambda g: g.dma_start(
                        dumps["d_qt"][:, :], bv("WSEG", qt0, qt0 + c.S)),
                        r=bk("WSEG", qt0, qt0 + c.S))
                    em.q2(lambda g: g.dma_start(
                        dumps["d_kt"][:, :], bv("WSEG", kt0, kt0 + c.S)),
                        r=bk("WSEG", kt0, kt0 + c.S))
                    em.q2(lambda g: g.dma_start(
                        dumps["d_vx"][:, :], bv("WSEG", vx0, vx0 + VW)),
                        r=bk("WSEG", vx0, vx0 + VW))
                attn_core(h, inject=(
                    (lambda h=h: emit_tiles(qk_all_ops(h + 1), width=3))
                    if h + 1 < c.HN else None))
                if h + 1 < c.HN:
                    emit_tiles(v_form_ops(h + 1))
            if c.dump and b == 0:
                em.q2(lambda g: g.dma_start(dumps["d_o"][:, :], bv("A2", 0, c.nT * c.D)),
                      r=bk("A2", 0, c.nT * c.D))

            if c.stop == "B":
                break
            WOv = lambda ch: AR[:, seg["WSEG"][0] + ch * c.D: seg["WSEG"][0] + (ch + 1) * c.D]
            WOk = lambda ch: kspan("AR", seg["WSEG"][0] + ch * c.D,
                                   seg["WSEG"][0] + (ch + 1) * c.D)
            for ch in range(c.nD):
                kscr = [("WOS", ch)]
                if b > 0:
                    em.q1(lambda g, ch=ch: g.dma_start(WOv(ch), wo_s[ch]),
                          r=kscr, w=WOk(ch))
                    continue
                wl = fv(WL0, WL0 + c.D)
                kwl = fk(WL0, WL0 + c.D)
                em.q1(lambda g, ch=ch, wl=wl: g.dma_start(
                    wl, wo_d[ch * 128:(ch + 1) * 128, :]), w=kwl)
                em.v(lambda v, ch=ch, wl=wl: v.tensor_copy(WOv(ch), wl), r=kwl, w=WOk(ch))
                em.q2(lambda g, ch=ch: g.dma_start(wo_s[ch], WOv(ch)), r=WOk(ch), w=kscr)
            OTST = lambda q: bv("A1", q * c.nD * 128, (q + 1) * c.nD * 128)
            for i in range(c.nT):
                q = i % 2
                for ch in range(c.nD):
                    bank = 6 + (ch % 2)
                    sa = i * c.D + ch * 128
                    em.t(lambda t, sa=sa, bank=bank: t.matmul(
                        PS(bank)[:, 0:128], bv("A2", sa, sa + 128), IDB,
                        start=True, stop=True), r=bk("A2", sa, sa + 128) + k_idb, w=PSK(bank))
                    da = q * c.nD * 128 + ch * 128
                    em.s(lambda s, da=da, bank=bank: s.copy(
                        bv("A1", da, da + 128), PS(bank)[:, 0:128]),
                        r=PSK(bank), w=bk("A1", da, da + 128))
                xr = XLv(q)
                em.q2(lambda g, i=i, xr=xr, b=b: g.dma_start(
                    xr, x_d[b, i * 128:(i + 1) * 128, :]), w=XLk(q))
                for db in range(c.nDB):
                    bank = 2 + (db % 2)
                    for ch in range(c.nD):
                        da = q * c.nD * 128 + ch * 128
                        em.t(lambda t, da=da, ch=ch, db=db, bank=bank: t.matmul(
                            PS(bank)[:, 0:512], bv("A1", da, da + 128),
                            WOv(ch)[:, db * 512:(db + 1) * 512],
                            start=(ch == 0), stop=(ch == c.nD - 1)),
                            r=bk("A1", da, da + 128) + WOk(ch), w=PSK(bank))
                    da2 = i * c.D + db * 512
                    em.v(lambda v, da2=da2, db=db, xr=xr, bank=bank: v.scalar_tensor_tensor(
                        bv("A2", da2, da2 + 512), PS(bank)[:, 0:512], 1.0,
                        xr[:, db * 512:(db + 1) * 512], OP.mult, OP.add),
                        r=PSK(bank) + XLk(q), w=bk("A2", da2, da2 + 512))
            if c.dump and b == 0:
                em.q2(lambda g: g.dma_start(dumps["d_x1"][:, :], bv("A2", 0, c.nT * c.D)),
                      r=bk("A2", 0, c.nT * c.D))

            if c.stop == "C":
                break
            for hf in range(2):
                tiles = list(range(hf * c.nTh, (hf + 1) * c.nTh))
                for ii, i in enumerate(tiles):
                    p = ii % 2
                    x1t = bv("A2", i * c.D, (i + 1) * c.D)
                    kx1t = bk("A2", i * c.D, (i + 1) * c.D)
                    mv, kmv, rstd, krstd = ln_stats(x1t, kx1t, p)
                    xn2 = fv(WL0, WL0 + c.D)
                    kxn2 = fk(WL0, WL0 + c.D)
                    em.v(lambda v, x1t=x1t, xn2=xn2, mv=mv, rstd=rstd: v.tensor_scalar(
                        xn2, x1t, mv[:, 0:1], rstd, OP.subtract, OP.mult),
                        r=kx1t + kmv + krstd, w=kxn2)
                    if c.stop == "D1":
                        continue
                    xtf = XLv(p)
                    for ch in range(c.nD):
                        bank = 6 + (ch % 2)
                        em.t(lambda t, xn2=xn2, ch=ch, bank=bank: t.matmul(
                            PS(bank)[:, 0:128], xn2[:, ch * 128:(ch + 1) * 128], IDF,
                            start=True, stop=True), r=kxn2 + k_idf, w=PSK(bank))
                        da = ch * c.Sh + ii * 128
                        em.s(lambda s, xtf=xtf, ch=ch, bank=bank: s.copy(
                            xtf[:, ch * 128:(ch + 1) * 128], PS(bank)[:, 0:128]),
                            r=PSK(bank), w=XLk(p))
                        em.v(lambda v, da=da, xtf=xtf, ch=ch: v.tensor_copy(
                            bv("A1", da, da + 128), xtf[:, ch * 128:(ch + 1) * 128]),
                            r=XLk(p), w=bk("A1", da, da + 128))
                        if c.stop == "D2":
                            continue
                        em.t(lambda t, xtf=xtf, ch=ch: t.matmul(
                            PS(5)[:, 0:8], xtf[:, ch * 128:(ch + 1) * 128], RWT(ch),
                            start=(ch == 0), stop=(ch == c.nD - 1)),
                            r=XLk(p) + k_rwt, w=PSK(5))
                    if c.stop in ("D2", "D3"):
                        continue
                    s0 = mi_row + 8 * (ii % 2)
                    negm = fv(s0, s0 + 1); knegm = fk(s0, s0 + 1)
                    m1e = fv(s0 + 1, s0 + 2); km1e = fk(s0 + 1, s0 + 2)
                    m2e = fv(s0 + 2, s0 + 3); km2e = fk(s0 + 2, s0 + 3)
                    srow = fv(s0 + 3, s0 + 4); ksrow = fk(s0 + 3, s0 + 4)
                    rcp = fv(s0 + 4, s0 + 5); krcp = fk(s0 + 4, s0 + 5)
                    e0 = mi_eg + 16 * (ii % 2)
                    eg = fv(e0, e0 + 8); keg = fk(e0, e0 + 8)
                    gm = fv(e0 + 8, e0 + 16); kgm = fk(e0 + 8, e0 + 16)
                    em.v(lambda v, negm=negm: v.reduce_max(
                        negm, PS(5)[:, 0:8], axis=mybir.AxisListType.X, negate=True),
                        r=PSK(5), w=knegm)
                    em.s(lambda s, eg=eg, negm=negm: s.activation(
                        eg, PS(5)[:, 0:8], AF.Exp, bias=negm),
                        r=PSK(5) + knegm, w=keg)
                    em.v(lambda v, m1e=m1e, eg=eg: v.reduce_max(
                        m1e, eg, axis=mybir.AxisListType.X), r=keg, w=km1e)
                    em.v(lambda v, gm=gm, eg=eg, m1e=m1e: v.tensor_scalar(
                        gm, eg, m1e, None, OP.is_ge), r=keg + km1e, w=kgm)
                    em.v(lambda v, gm=gm, eg=eg: v.scalar_tensor_tensor(
                        gm, gm, -1e30, eg, OP.mult, OP.add), r=kgm + keg, w=kgm)
                    em.v(lambda v, m2e=m2e, gm=gm: v.reduce_max(
                        m2e, gm, axis=mybir.AxisListType.X), r=kgm, w=km2e)
                    gv = gates_v(i)
                    gk = gates_k(i)
                    em.v(lambda v, gv=gv, eg=eg, m2e=m2e: v.tensor_scalar(
                        gv, eg, m2e, None, OP.is_ge), r=keg + km2e, w=gk)
                    em.v(lambda v, gv=gv, eg=eg: v.tensor_tensor(
                        gv, gv, eg, OP.mult), r=gk + keg, w=gk)
                    em.v(lambda v, srow=srow, gv=gv: v.reduce_sum(
                        srow, gv, axis=mybir.AxisListType.X), r=gk, w=ksrow)
                    em.v(lambda v, rcp=rcp, srow=srow: v.reciprocal(rcp, srow),
                         r=ksrow, w=krcp)
                    em.v(lambda v, gv=gv, rcp=rcp: v.tensor_scalar(
                        gv, gv, rcp, None, OP.mult), r=gk + krcp, w=gk)
                if c.dump and b == 0 and hf == 1:
                    em.q2(lambda g: g.dma_start(
                        dumps["d_gates"][:, :],
                        bv("CONST", g0, g0 + c.nT * 16).bitcast(F32)), r=k_gates_all)

                if c.stop == "D":
                    continue
                kxn2T = bk("A1", 0, c.nD * c.Sh)
                pes = [("e", e, c.nHF) for e in range(c.E)]
                pes += [("s", 0, c.nHS // 2), ("s", 1, c.nHS - c.nHS // 2)]
                up0 = 0
                wd0 = 4 * (c.nD * 128)
                first_ffn = (b == 0 and hf == 0)
                for pei, (kind, e, nhc) in enumerate(pes):
                    spe = pei
                    for hc in range(nhc):
                        upar = (pei * 16 + hc) % 2
                        gbuf = up0 + upar * 2 * (c.nD * 128)
                        ubuf = gbuf + c.nD * 128
                        for (srcg, scrd, dst0) in (((wg_d, sg_d), wg_s, gbuf),
                                                   ((wu_d, su_d), wu_s, ubuf)):
                            kdst = bk("WSEG", dst0, dst0 + c.nD * 128)
                            kscr = [("WGS" if scrd is wg_s else "WUS", spe, hc)]
                            if not first_ffn:
                                em.q1(lambda g, scrd=scrd, spe=spe, hc=hc, dst0=dst0:
                                      g.dma_start(bv("WSEG", dst0, dst0 + c.nD * 128),
                                                  scrd[spe, hc]), r=kscr, w=kdst)
                                continue
                            wl = fv(WL0, WL0 + c.nD * 128)
                            kwl = fk(WL0, WL0 + c.nD * 128)
                            if kind == "e":
                                src = srcg[0][e]
                                h0 = hc * 128
                            else:
                                src = srcg[1]
                                h0 = (e * (c.nHS // 2) + hc) * 128
                            em.q1(lambda g, src=src, h0=h0, wl=wl: g.dma_start(
                                wl.rearrange("p (ch h) -> p ch h", h=128),
                                src.rearrange("(ch p) h -> p ch h", p=128)[:, :, h0:h0 + 128]),
                                w=kwl)
                            em.v(lambda v, wl=wl, dst0=dst0: v.tensor_copy(
                                bv("WSEG", dst0, dst0 + c.nD * 128), wl),
                                r=kwl, w=kdst)
                            em.q2(lambda g, scrd=scrd, spe=spe, hc=hc, dst0=dst0:
                                  g.dma_start(scrd[spe, hc],
                                              bv("WSEG", dst0, dst0 + c.nD * 128)),
                                  r=kdst, w=kscr)
                        kgb = bk("WSEG", gbuf, gbuf + c.nD * 128)
                        kub = bk("WSEG", ubuf, ubuf + c.nD * 128)
                        for j, (ja, jb) in enumerate(nsplit(0, c.Sh, 512)):
                            jn = jb - ja
                            gbank = (j % 2)
                            ubank = 2 + (j % 2)
                            for ch in range(c.nD):
                                em.t(lambda t, ch=ch, ja=ja, jb=jb, jn=jn, gbuf=gbuf,
                                     gbank=gbank: t.matmul(
                                    PS(gbank)[:, 0:jn],
                                    bv("WSEG", gbuf + ch * 128, gbuf + (ch + 1) * 128),
                                    bv("A1", ch * c.Sh + ja, ch * c.Sh + jb),
                                    start=(ch == 0), stop=(ch == c.nD - 1)),
                                    r=bk("WSEG", gbuf + ch * 128, gbuf + (ch + 1) * 128)
                                      + bk("A1", ch * c.Sh + ja, ch * c.Sh + jb),
                                    w=PSK(gbank))
                            for ch in range(c.nD):
                                em.t(lambda t, ch=ch, ja=ja, jb=jb, jn=jn, ubuf=ubuf,
                                     ubank=ubank: t.matmul(
                                    PS(ubank)[:, 0:jn],
                                    bv("WSEG", ubuf + ch * 128, ubuf + (ch + 1) * 128),
                                    bv("A1", ch * c.Sh + ja, ch * c.Sh + jb),
                                    start=(ch == 0), stop=(ch == c.nD - 1)),
                                    r=bk("WSEG", ubuf + ch * 128, ubuf + (ch + 1) * 128)
                                      + bk("A1", ch * c.Sh + ja, ch * c.Sh + jb),
                                    w=PSK(ubank))
                            silst = bv("BQ", sil0 + (j % 2) * 512, sil0 + (j % 2) * 512 + jn)
                            ksil = bk("BQ", sil0 + (j % 2) * 512, sil0 + (j % 2) * 512 + 512)
                            em.s(lambda s, silst=silst, gbank=gbank, jn=jn: s.activation(
                                silst, PS(gbank)[:, 0:jn], AF.Silu), r=PSK(gbank), w=ksil)
                            hda = hc * c.Sh + ja
                            em.v(lambda v, hda=hda, silst=silst, ubank=ubank, jn=jn:
                                 v.tensor_tensor(
                                     bv("BQ", hda, hda + jn), silst, PS(ubank)[:, 0:jn],
                                     OP.mult),
                                 r=ksil + PSK(ubank), w=bk("BQ", hda, hda + jn))
                    if c.dump and b == 0 and hf == 1 and pei == 0:
                        em.q2(lambda g: g.dma_start(dumps["d_h0"][:, :],
                                                    bv("BQ", 0, c.nHF * c.Sh)),
                              r=bk("BQ", 0, c.nHF * c.Sh))
                    assert c.nTh <= 8
                    for db in range(c.nDB):
                        for hg0 in range(0, nhc, 4):
                            hgn = min(4, nhc - hg0)
                            gpar = (hg0 // 4) % 2
                            slot = wd0 + gpar * 2048
                            kslot = bk("WSEG", slot, slot + hgn * 512)
                            kscr = [("WDS", spe, db, hg0)]
                            if not first_ffn:
                                em.q1(lambda g, spe=spe, db=db, hg0=hg0, hgn=hgn, slot=slot:
                                      g.dma_start(
                                          bv("WSEG", slot, slot + hgn * 512).rearrange(
                                              "p (h d) -> p h d", d=512),
                                          wd_s[spe, db, hg0:hg0 + hgn].rearrange(
                                              "h p d -> p h d")),
                                      r=kscr, w=kslot)
                            else:
                                for hx in range(hgn):
                                    hc = hg0 + hx
                                    wl = fv(WL0 + c.D + (hx % 2) * 512,
                                            WL0 + c.D + (hx % 2 + 1) * 512)
                                    kwl = fk(WL0 + c.D + (hx % 2) * 512,
                                             WL0 + c.D + (hx % 2 + 1) * 512)
                                    if kind == "e":
                                        src = wd_d[e]
                                        r0 = hc * 128
                                    else:
                                        src = sd_d
                                        r0 = (e * (c.nHS // 2) + hc) * 128
                                    em.q1(lambda g, src=src, r0=r0, db=db, wl=wl: g.dma_start(
                                        wl, src[r0:r0 + 128, db * 512:(db + 1) * 512]), w=kwl)
                                    da = slot + hx * 512
                                    em.v(lambda v, wl=wl, da=da: v.tensor_copy(
                                        bv("WSEG", da, da + 512), wl),
                                        r=kwl, w=bk("WSEG", da, da + 512))
                                em.q2(lambda g, spe=spe, db=db, hg0=hg0, hgn=hgn, slot=slot:
                                      g.dma_start(
                                          wd_s[spe, db, hg0:hg0 + hgn].rearrange(
                                              "h p d -> p h d"),
                                          bv("WSEG", slot, slot + hgn * 512).rearrange(
                                              "p (h d) -> p h d", d=512)),
                                      r=kslot, w=kscr)
                            for hx in range(hgn):
                                hc = hg0 + hx
                                da = slot + hx * 512
                                for kk in range(c.nTh):
                                    hsa = hc * c.Sh + kk * 128
                                    em.t(lambda t, hsa=hsa, da=da, kk=kk, hc=hc, nhc=nhc:
                                         t.matmul(PS(kk)[:, 0:512], bv("BQ", hsa, hsa + 128),
                                                  bv("WSEG", da, da + 512),
                                                  start=(hc == 0), stop=(hc == nhc - 1)),
                                         r=bk("BQ", hsa, hsa + 128) + bk("WSEG", da, da + 512),
                                         w=PSK(kk))
                        for kk in range(c.nTh):
                            i = hf * c.nTh + kk
                            x1sl = i * c.D + db * 512
                            kx1 = bk("A2", x1sl, x1sl + 512)
                            if kind == "e":
                                gsc = gates_v(i)[:, e:e + 1]
                                kg = gates_k(i)
                            else:
                                gsc = 1.0
                                kg = []
                            em.v(lambda v, x1sl=x1sl, gsc=gsc, kk=kk:
                                 v.scalar_tensor_tensor(
                                     bv("A2", x1sl, x1sl + 512), PS(kk)[:, 0:512], gsc,
                                     bv("A2", x1sl, x1sl + 512), OP.mult, OP.add),
                                 r=PSK(kk) + kg + kx1, w=kx1)
                for kk in range(c.nTh):
                    i = hf * c.nTh + kk
                    op_ = XLv(kk % 2)
                    em.v(lambda v, i=i, op_=op_: v.tensor_copy(
                        op_, bv("A2", i * c.D, (i + 1) * c.D)),
                        r=bk("A2", i * c.D, (i + 1) * c.D), w=XLk(kk % 2))
                    em.q2(lambda g, i=i, op_=op_, b=b: g.dma_start(
                        out_d[b, i * 128:(i + 1) * 128, :], op_),
                        r=XLk(kk % 2), w=[("OUT", n_out_dma[0])])
                    n_out_dma[0] += 1

        if n_out_dma[0] == 0:
            em.v(lambda v: v.memset(XLv(0), 0.0), w=XLk(0))
            em.q2(lambda g: g.dma_start(out_d[0, 0:128, :], XLv(0)),
                  r=XLk(0), w=[("OUT", 0)])
            n_out_dma[0] = 1
        em.q1(lambda g: g.nop(), r=[("OUT", j) for j in range(n_out_dma[0])])

        counts = em.finalize(blk, sems)
    return nc, dumps


_FULL = CFG()


def kernel(L0_attn_x, L0_attn_complex_freq, L0_attn_W_Aq, L0_attn_W_Ak, L0_attn_W_Av,
           L0_attn_W_Bq, L0_attn_W_Bk, L0_attn_W_Bv, L0_attn_Wo, L0_ffn_router_w,
           L0_ffn_wg, L0_ffn_wu, L0_ffn_wd, L0_ffn_sg, L0_ffn_su, L0_ffn_sd):
    import ml_dtypes
    c = _FULL
    nc, _ = build_program(c)
    x = np.ascontiguousarray(np.asarray(L0_attn_x, dtype=np.float32))
    common = {
        "cfreq": np.ascontiguousarray(L0_attn_complex_freq, dtype=np.float32),
        "wAq": np.ascontiguousarray(L0_attn_W_Aq, dtype=np.float32),
        "wAk": np.ascontiguousarray(L0_attn_W_Ak, dtype=np.float32),
        "wAv": np.ascontiguousarray(L0_attn_W_Av, dtype=np.float32),
        "wBq": np.ascontiguousarray(L0_attn_W_Bq, dtype=np.float32),
        "wBk": np.ascontiguousarray(L0_attn_W_Bk, dtype=np.float32),
        "wBv": np.ascontiguousarray(L0_attn_W_Bv, dtype=np.float32),
        "wo": np.ascontiguousarray(L0_attn_Wo, dtype=np.float32),
        "rw": np.ascontiguousarray(L0_ffn_router_w, dtype=np.float32),
        "wg": np.ascontiguousarray(L0_ffn_wg, dtype=np.float32),
        "wu": np.ascontiguousarray(L0_ffn_wu, dtype=np.float32),
        "wd": np.ascontiguousarray(L0_ffn_wd, dtype=np.float32),
        "sg": np.ascontiguousarray(L0_ffn_sg, dtype=np.float32),
        "su": np.ascontiguousarray(L0_ffn_su, dtype=np.float32),
        "sd": np.ascontiguousarray(L0_ffn_sd, dtype=np.float32),
        "identb": np.eye(128, dtype=ml_dtypes.bfloat16),
        "identf": np.eye(128, dtype=np.float32),
    }
    n_cores = 8
    in_maps = []
    for ci in range(n_cores):
        m = dict(common)
        m["x"] = x[ci * c.B_LOC:(ci + 1) * c.B_LOC]
        in_maps.append(m)
    res = run_bass_kernel_spmd(nc, in_maps, core_ids=list(range(n_cores)))
    return np.concatenate([r["out"] for r in res.results], axis=0)
